# version 05
# baseline (speedup 1.0000x reference)
import math
import numpy as np
import ml_dtypes
import concourse.bass as bass
import concourse.mybir as mybir
from concourse.bass_utils import run_bass_kernel_spmd

F32 = mybir.dt.float32
BF16 = mybir.dt.bfloat16
AF = mybir.ActivationFunctionType
ALU = mybir.AluOpType
AX = mybir.AxisListType

D = 2048
NDC = 16
T = 1024
TB = 512
NTB = 2
E = 16
NFC = 6
NH = 16
NQT = 4
NQ = 256
RSCALE = 1.0 / math.sqrt(128.0)
EPS = 1e-5

ENGS = ("tensor", "scalar", "vector", "gpsimd", "sync")


class B:

    def __init__(self, nc):
        self.nc = nc
        self.q = {e: [] for e in ENGS}
        self.sem = {}
        self.cnt = {}
        self.waited = {e: {} for e in ENGS}

    def mksem(self, name):
        self.sem[name] = self.nc.alloc_semaphore(name)
        self.cnt[name] = 0

    def emit(self, eng, fn):
        self.q[eng].append(fn)

    def wait(self, eng, sname, val):
        if val is None or val <= 0:
            return
        assert val <= self.cnt[sname], (
            f"{eng} waits {sname}>={val} but only {self.cnt[sname]} emitted")
        if self.waited[eng].get(sname, 0) >= val:
            return
        self.waited[eng][sname] = val
        s = self.sem[sname]
        self.emit(eng, lambda E, s=s, v=val: E.wait_ge(s, v))

    SELF_SYNC = {"vector": "PDVE", "scalar": "PACT"}

    def step(self, eng, sname, fn, inc=1):
        own = self.SELF_SYNC.get(eng)
        if own is not None and self.cnt[own] > 0:
            self.wait(eng, own, self.cnt[own])
        self.cnt[sname] += inc
        s = self.sem[sname]
        self.emit(eng, lambda E, fn=fn, s=s, inc=inc: fn(E).then_inc(s, inc))
        return self.cnt[sname]

    def run(self, eng_name, E):
        for fn in self.q[eng_name]:
            fn(E)


class Slots:

    def __init__(self, n):
        self.n = n
        self.free = [None] * n

    def slot(self, i):
        return i % self.n

    def prod_wait(self, i):
        return self.free[i % self.n]

    def consumed(self, i, sname, val):
        self.free[i % self.n] = (sname, val)


def build():
    nc = bass.Bass("TRN2", target_bir_lowering=False)

    x_in = nc.dram_tensor("x", [128, NDC * T], F32, kind="ExternalInput")
    wqkv_in = nc.dram_tensor("wqkv", [48, 128, 2048], F32, kind="ExternalInput")
    wo_in = nc.dram_tensor("wo", [16, 128, 2048], F32, kind="ExternalInput")
    gate_in = nc.dram_tensor("gate", [16, 6, 128, 2048], F32, kind="ExternalInput")
    up_in = nc.dram_tensor("up", [16, 6, 128, 2048], F32, kind="ExternalInput")
    down_in = nc.dram_tensor("down", [16, 16, 128, 768], F32, kind="ExternalInput")
    router_in = nc.dram_tensor("router", [128, 256], F32, kind="ExternalInput")
    bqkv_in = nc.dram_tensor("bqkv", [128, 48], F32, kind="ExternalInput")
    bo_in = nc.dram_tensor("bo", [128, 16], F32, kind="ExternalInput")
    ident_in = nc.dram_tensor("identw", [128, 128], F32, kind="ExternalInput")
    masks_in = nc.dram_tensor("masks", [128, 2048], BF16, kind="ExternalInput")
    out_t = nc.dram_tensor("out", [128, NDC * T], F32, kind="ExternalOutput")
    kv_contrib = nc.dram_tensor("kvc", [4096, T], BF16, kind="Internal")
    kvgK = nc.dram_tensor("kvgK", [4 * 2048, T], BF16, kind="Internal")
    kvgV = nc.dram_tensor("kvgV", [4 * 2048, T], BF16, kind="Internal")
    q_dram = nc.dram_tensor("qd", [2048, T], BF16, kind="Internal")
    ct_dram = nc.dram_tensor("ctd", [2, 16, T], F32, kind="Internal")

    A = nc.alloc_sbuf_tensor
    x_sb = A("x_sb", [128, NDC * T], F32)
    xn = A("xn", [128, NDC * T], BF16)
    h_sb = A("h_sb", [128, NFC * NTB * TB], BF16)
    wbuf = [A(f"wbuf{i}", [128, 2048], BF16) for i in range(3)]
    dbuf = [A(f"dbuf{i}", [128, 768], BF16) for i in range(4)]
    kbuf = [A(f"kbuf{i}", [128, 4096], BF16) for i in range(2)]
    vbuf = [A(f"vbuf{i}", [128, 4096], BF16) for i in range(2)]
    qbuf = [A(f"qbuf{i}", [128, 1024], BF16) for i in range(2)]
    ppool = A("ppool", [128, 4 * NQ], BF16)
    mask_sb = A("mask_sb", [128, 2048], BF16)
    xbp = A("xbp", [128, 2 * TB], BF16)
    xqp = A("xqp", [128, 2 * TB], BF16)
    fsc = A("fsc", [128, 3 * TB], F32)
    stg = A("stg", [128, 4 * TB], BF16)
    zbs = A("zbs", [128, 2 * NQ], F32)
    router_sb = A("router_sb", [128, 512], F32)
    R_mr = A("R_mr", [128, 4 * TB], BF16)
    R_ce = A("R_ce", [128, 2 * TB], BF16)
    R_rz = A("R_rz", [128, 2 * NQ], F32)
    smallp = A("smallp", [128, 256], F32)
    ct_sb = A("ct_sb", [16, T], F32)
    onesf = A("onesf", [128, 256], F32)
    onesb = A("onesb", [128, 128], BF16)
    ones_cb = A("ones_cb", [128, 1], BF16)
    ones_cf = A("ones_cf", [128, 1], F32)
    rowp = A("rowp", [128, 1536], F32)
    colsp = A("colsp", [128, 16], F32)
    bq_sb = A("bq_sb", [128, 48], F32)
    bqs_sb = A("bqs_sb", [128, 16], F32)
    bo_sb = A("bo_sb", [128, 16], F32)

    bankt = [nc.alloc_psum_tensor(f"bank{i}", [128, 512], F32) for i in range(8)]

    b = B(nc)
    for s in ("PPE", "PACT", "PDVE", "DIN",
              "WS0", "WS1", "WS2", "DS0", "DS1", "DS2", "DS3",
              "CE0", "CE1", "SG0", "SG1", "SG2", "SG3", "CT",
              "KQ0", "KQ1", "KQ2", "KQ3", "KK0", "KK1", "KK2", "KK3",
              "KV0", "KV1", "KV2", "KV3", "DX0", "DX1", "DX2", "DX3",
              "KH0", "KH1", "CCK", "CCV", "DO"):
        b.mksem(s)

    bank_rd = [None] * 8

    def bank_wait(eng, i):
        if bank_rd[i] is not None:
            b.wait(eng, *bank_rd[i])

    def pe(fn, wait=(), bankw=()):
        for w in wait:
            b.wait("tensor", *w)
        for i in bankw:
            bank_wait("tensor", i)
        return b.step("tensor", "PPE", fn)

    def act(fn, wait=(), bankw=()):
        for w in wait:
            b.wait("scalar", *w)
        for i in bankw:
            bank_wait("scalar", i)
        return b.step("scalar", "PACT", fn)

    def dve(fn, wait=(), bankw=()):
        for w in wait:
            b.wait("vector", *w)
        for i in bankw:
            bank_wait("vector", i)
        return b.step("vector", "PDVE", fn)

    def mm_group(psum_ap, ops, wait=(), bankw=()):
        for w in wait:
            b.wait("tensor", *w)
        for i in bankw:
            bank_wait("tensor", i)
        n = len(ops)
        v = None
        for j, (l, r) in enumerate(ops):
            fn = (lambda E, o=psum_ap, l=l, r=r, st=(j == 0), sp=(j == n - 1):
                  E.matmul(o, l, r, start=st, stop=sp))
            if j == n - 1:
                v = b.step("tensor", "PPE", fn)
            else:
                b.emit("tensor", fn)
        return v

    def sy_dma(sname, dst, src, wait=()):
        for w in wait:
            b.wait("sync", *w)
        return b.step("sync", sname, lambda E, d=dst, s=src: E.dma_start(d, s), 16)

    def sc_dma(sname, dst, src, wait=(), transpose=False):
        for w in wait:
            b.wait("scalar", *w)
        return b.step("scalar", sname,
                      lambda E, d=dst, s=src, t=transpose: E.dma_start(d, s, transpose=t),
                      16)

    def sy_dma_t(sname, dst, src, wait=()):
        for w in wait:
            b.wait("sync", *w)
        return b.step("sync", sname,
                      lambda E, d=dst, s=src: E.dma_start(d, s, transpose=True), 16)

    def gp_dma(sname, dst, src, wait=()):
        for w in wait:
            b.wait("gpsimd", *w)
        return b.step("gpsimd", sname, lambda E, d=dst, s=src: E.dma_start(d, s), 16)

    def xs(dc, tb):
        return x_sb[:, dc * T + tb * TB: dc * T + (tb + 1) * TB]

    def xns(dc, tb):
        return xn[:, dc * T + tb * TB: dc * T + (tb + 1) * TB]

    for j in range(4):
        sy_dma(f"DX{j}", x_sb[:, j * 4096:(j + 1) * 4096],
               x_in[:, j * 4096:(j + 1) * 4096])
    sy_dma("DIN", router_sb[:, 0:256], router_in[:, :])
    sy_dma("DIN", bq_sb[:, :], bqkv_in[:, :])
    sy_dma("DIN", bo_sb[:, :], bo_in[:, :])
    sy_dma("DIN", onesf[:, 128:256], ident_in[:, :])
    sy_dma("DIN", mask_sb[:, :], masks_in[:, :])
    DIN_ALL = b.cnt["DIN"]

    for fn in (
        lambda E: E.memset(onesf[:, 0:128], 1.0),
        lambda E: E.memset(onesb[:, :], 1.0),
        lambda E: E.memset(ones_cb[:, :], 1.0),
        lambda E: E.memset(ones_cf[:, :], 1.0),
        lambda E: E.memset(R_mr[:, :], 0.0),
        lambda E: E.memset(R_ce[:, :], 0.0),
        lambda E: E.memset(R_rz[:, :], 0.0),
    ):
        dve(fn)
    INIT = dve(lambda E: E.memset(smallp[:, :], 0.0))
    BQS = dve(lambda E: E.tensor_scalar(bqs_sb[:, :], bq_sb[:, 0:16], RSCALE, None,
                                        ALU.mult),
              wait=[("DIN", DIN_ALL)])

    CSR = mm_group(bankt[7][0:1, 0:16],
                   [(ones_cf[:, :], router_sb[:, dc * 16:dc * 16 + 16])
                    for dc in range(NDC)],
                   wait=[("DIN", DIN_ALL), ("PDVE", INIT)])
    CSRROW = dve(lambda E: E.tensor_copy(smallp[0:1, 0:16], bankt[7][0:1, 0:16]),
                 wait=[("PPE", CSR)])
    bank_rd[7] = ("PDVE", CSRROW)
    CSRB_MM = pe(lambda E: E.matmul(bankt[6][:, 0:16], onesf[:, 0:128],
                                    smallp[:, 0:16], start=True, stop=True),
                 wait=[("PDVE", CSRROW)])
    CSRB = dve(lambda E: E.tensor_copy(smallp[:, 32:48], bankt[6][:, 0:16]),
               wait=[("PPE", CSRB_MM)])
    bank_rd[6] = ("PDVE", CSRB)
    csrb = smallp[:, 32:48]

    xb_ring = Slots(2)
    xq_ring = Slots(2)
    fsc_ring = Slots(3)
    stg_ring = Slots(3)
    fci = [0]
    ln_count = [0]

    def ln(xdep_dve=None, xdep_map=None, want_router=False, call_idx=0):
        li = ln_count[0]
        ln_count[0] += 1

        def xdep(dc, tb):
            if xdep_map is not None:
                return xdep_map[(dc, tb)]
            return xdep_dve
        xn_done = [None, None]
        mr_rows = {}
        for tb in range(NTB):
            for dc in range(NDC):
                i = tb * NDC + dc
                s = xb_ring.slot(i)
                w = [(f"DX{dc // 4}", 16)]
                if xdep(dc, tb):
                    w.append(("PDVE", xdep(dc, tb)))
                pw = xb_ring.prod_wait(i)
                if pw:
                    w.append(pw)
                xbv = dve(lambda E, d=dc, t=tb, s=s: E.tensor_copy(
                    xbp[:, s * TB:(s + 1) * TB], xs(d, t)), wait=w)
                wq = [(f"DX{dc // 4}", 16)]
                if xdep(dc, tb):
                    wq.append(("PDVE", xdep(dc, tb)))
                pwq = xq_ring.prod_wait(i)
                if pwq:
                    wq.append(pwq)
                xqv = act(lambda E, d=dc, t=tb, s=s: E.activation(
                    xqp[:, s * TB:(s + 1) * TB], xs(d, t), AF.Square), wait=wq)
                bw = [0] if dc == 0 else []
                v1 = pe(lambda E, s=s, st=(dc == 0), sp=(dc == NDC - 1):
                        E.matmul(bankt[0][0:1, 0:TB], ones_cb[:, :],
                                 xbp[:, s * TB:(s + 1) * TB], start=st, stop=sp),
                        wait=[("PDVE", xbv)], bankw=bw)
                v2 = pe(lambda E, s=s, st=(dc == 0), sp=(dc == NDC - 1):
                        E.matmul(bankt[1][0:1, 0:TB], ones_cb[:, :],
                                 xqp[:, s * TB:(s + 1) * TB], start=st, stop=sp),
                        wait=[("PACT", xqv)], bankw=[1] if dc == 0 else [])
                xb_ring.consumed(i, "PPE", v1)
                xq_ring.consumed(i, "PPE", v2)
            S_MM = v2
            MF = dve(lambda E: E.tensor_scalar(rowp[0:1, 0:TB], bankt[0][0:1, 0:TB],
                                               1.0 / D, None, ALU.mult),
                     wait=[("PPE", S_MM)])
            bank_rd[0] = ("PDVE", MF)
            MROW = dve(lambda E, t=tb: E.tensor_copy(R_mr[0:1, t * TB:(t + 1) * TB],
                                                     rowp[0:1, 0:TB]))
            S2S = dve(lambda E: E.tensor_scalar(rowp[0:1, TB:2 * TB],
                                                bankt[1][0:1, 0:TB],
                                                1.0 / D, None, ALU.mult))
            bank_rd[1] = ("PDVE", S2S)
            M2 = dve(lambda E: E.tensor_mul(rowp[0:1, 2 * TB:3 * TB],
                                            rowp[0:1, 0:TB], rowp[0:1, 0:TB]))
            VV = dve(lambda E: E.tensor_sub(rowp[0:1, TB:2 * TB],
                                            rowp[0:1, TB:2 * TB],
                                            rowp[0:1, 2 * TB:3 * TB]))
            VE = dve(lambda E: E.tensor_scalar(rowp[0:1, TB:2 * TB],
                                               rowp[0:1, TB:2 * TB], EPS, None,
                                               ALU.add))
            STD = act(lambda E: E.activation(rowp[0:1, 0:TB],
                                             rowp[0:1, TB:2 * TB], AF.Sqrt),
                      wait=[("PDVE", VE)])
            RI = dve(lambda E: E.reciprocal(rowp[0:1, 2 * TB:3 * TB],
                                            rowp[0:1, 0:TB]),
                     wait=[("PACT", STD)])
            RROW = dve(lambda E, t=tb: E.tensor_copy(
                R_mr[0:1, (2 + t) * TB:(3 + t) * TB], rowp[0:1, 2 * TB:3 * TB]))
            mr_rows[tb] = (MROW, RROW)
            MB = pe(lambda E, t=tb: E.matmul(bankt[2][:, 0:TB], onesb[:, :],
                                             R_mr[:, t * TB:(t + 1) * TB],
                                             start=True, stop=True),
                    wait=[("PDVE", MROW)], bankw=[2])
            RB = pe(lambda E, t=tb: E.matmul(bankt[3][:, 0:TB], onesb[:, :],
                                             R_mr[:, (2 + t) * TB:(3 + t) * TB],
                                             start=True, stop=True),
                    wait=[("PDVE", RROW)], bankw=[3])
            for dc in range(NDC):
                i = fci[0]
                fci[0] += 1
                s = fsc_ring.slot(i)
                w = [("PPE", MB), (f"DX{dc // 4}", 16)]
                pw = fsc_ring.prod_wait(i)
                if pw:
                    w.append(pw)
                TT = dve(lambda E, d=dc, t=tb, s=s: E.tensor_sub(
                    fsc[:, s * TB:(s + 1) * TB], xs(d, t), bankt[2][:, 0:TB]),
                    wait=w)
                XN = dve(lambda E, d=dc, t=tb, s=s: E.tensor_mul(
                    xns(d, t), fsc[:, s * TB:(s + 1) * TB], bankt[3][:, 0:TB]),
                    wait=[("PPE", RB)])
                fsc_ring.consumed(i, "PDVE", XN)
            xn_done[tb] = XN
            bank_rd[2] = ("PDVE", XN)
            bank_rd[3] = ("PDVE", XN)

        if want_router:
            xa = (max(xdep_map.values()) if xdep_map is not None else xdep_dve)
            router(call_idx, xa, mr_rows)
        return xn_done

    def router(call_idx, xdep_all, mr_rows):
        rt = [router_sb[:, 256 + k * 16: 256 + (k + 1) * 16] for k in range(4)]
        for tt in range(8):
            tb = tt // 4
            rb_i = 4 + (tt % 2)
            w = [("DIN", DIN_ALL)] + [(f"DX{j}", 16) for j in range(4)]
            if xdep_all:
                w.append(("PDVE", xdep_all))
            RAW = mm_group(
                bankt[rb_i][:, 0:16],
                [(x_sb[:, dc * T + tt * 128: dc * T + (tt + 1) * 128],
                  router_sb[:, dc * 16:dc * 16 + 16]) for dc in range(NDC)],
                wait=w, bankw=[rb_i])
            MROW, RROW = mr_rows[tb]
            MC = pe(lambda E, t=tb, q=tt % 4: E.matmul(
                bankt[6][:, 0:1],
                R_mr[:, t * TB + q * 128: t * TB + (q + 1) * 128],
                onesb[:, 0:1], start=True, stop=True),
                wait=[("PDVE", MROW)], bankw=[6])
            RC = pe(lambda E, t=tb, q=tt % 4: E.matmul(
                bankt[6][:, 1:2],
                R_mr[:, (2 + t) * TB + q * 128: (2 + t) * TB + (q + 1) * 128],
                onesb[:, 0:1], start=True, stop=True),
                wait=[("PDVE", RROW)])
            MS = dve(lambda E: E.tensor_scalar(rt[0], csrb, bankt[6][0:128, 0:1],
                                               None, ALU.mult),
                     wait=[("PPE", MC)])
            S0 = dve(lambda E, i=rb_i: E.scalar_tensor_tensor(
                rt[1], bankt[i][:, 0:16], 1.0, rt[0], ALU.mult, ALU.subtract),
                wait=[("PPE", RAW)])
            bank_rd[rb_i] = ("PDVE", S0)
            S1 = dve(lambda E: E.tensor_scalar(rt[1], rt[1], bankt[6][0:128, 1:2],
                                               None, ALU.mult),
                     wait=[("PPE", RC)])
            bank_rd[6] = ("PDVE", S1)
            M1 = dve(lambda E: E.reduce_max(colsp[:, 0:1], rt[1], axis=AX.X))
            MK1 = dve(lambda E: E.tensor_scalar(rt[2], rt[1], colsp[:, 0:1], None,
                                                ALU.is_equal))
            TM = dve(lambda E: E.tensor_scalar(rt[3], rt[2], -1e30, None, ALU.mult))
            TM2 = dve(lambda E: E.tensor_add(rt[3], rt[3], rt[1]))
            M2_ = dve(lambda E: E.reduce_max(colsp[:, 1:2], rt[3], axis=AX.X))
            MK2 = dve(lambda E: E.tensor_scalar(rt[3], rt[1], colsp[:, 1:2], None,
                                                ALU.is_equal))
            DL = dve(lambda E: E.tensor_sub(colsp[:, 2:3], colsp[:, 0:1],
                                            colsp[:, 1:2]))
            W1 = act(lambda E: E.activation(colsp[:, 3:4], colsp[:, 2:3],
                                            AF.Sigmoid),
                     wait=[("PDVE", DL)])
            W1H = dve(lambda E: E.tensor_scalar(colsp[:, 4:5], colsp[:, 3:4], 0.5,
                                                None, ALU.mult),
                      wait=[("PACT", W1)])
            W2H = dve(lambda E: E.tensor_scalar(colsp[:, 5:6], colsp[:, 3:4], -0.5,
                                                0.5, ALU.mult, ALU.add))
            C1 = dve(lambda E: E.tensor_scalar(rt[2], rt[2], colsp[:, 4:5], None,
                                               ALU.mult))
            C2 = dve(lambda E: E.tensor_scalar(rt[3], rt[3], colsp[:, 5:6], None,
                                               ALU.mult))
            CC_ = dve(lambda E: E.tensor_add(rt[2], rt[2], rt[3]))
            TR = pe(lambda E: E.transpose(bankt[7][0:16, 0:128], rt[2],
                                          onesf[:, 128:256]),
                    wait=[("PDVE", CC_)], bankw=[7])
            ctw = [("PPE", TR)]
            if call_idx > 0:
                ctw.append(("CT", 16 * call_idx))
            CT_EV = dve(lambda E, q=tt: E.tensor_copy(ct_sb[:, q * 128:(q + 1) * 128],
                                                      bankt[7][0:16, 0:128]),
                        wait=ctw)
            bank_rd[7] = ("PDVE", CT_EV)
        sy_dma("CT", ct_dram[call_idx, :, :], ct_sb[:, :],
               wait=[("PDVE", CT_EV)])

    wring = Slots(3)
    dring = Slots(4)
    ce_ring = Slots(2)
    wi = [0]
    di = [0]
    cei = [0]

    def wslab_load(src_ap):
        i = wi[0]
        wi[0] += 1
        s = wring.slot(i)
        pw = wring.prod_wait(i)
        v = gp_dma(f"WS{s}", wbuf[s][:, :], src_ap, wait=[pw] if pw else [])
        return s, v

    def dslab_load(src_ap):
        i = di[0]
        di[0] += 1
        s = dring.slot(i)
        pw = dring.prod_wait(i)
        v = gp_dma(f"DS{s}", dbuf[s][:, 0:768], src_ap, wait=[pw] if pw else [])
        return s, v

    def ffn(call_idx, xn_done, xprev_pe=None):
        ct_need = 16 * (call_idx + 1)
        xadd_map = {}
        for e in range(E):
            cbv = {}
            for tb in range(NTB):
                i = cei[0]
                cei[0] += 1
                s = ce_ring.slot(i)
                pw = ce_ring.prod_wait(i)
                w = [("CT", ct_need)] + ([pw] if pw else [])
                cev = gp_dma(f"CE{s}", R_ce[0:1, s * TB:(s + 1) * TB],
                             ct_dram[call_idx, e, tb * TB:(tb + 1) * TB], wait=w)
                CB = pe(lambda E_, s=s, t=tb: E_.matmul(
                    bankt[4 + t][:, 0:TB], onesb[:, :],
                    R_ce[:, s * TB:(s + 1) * TB], start=True, stop=True),
                    wait=[(f"CE{s}", cev)], bankw=[4 + tb])
                ce_ring.consumed(i, "PPE", CB)
                cbv[tb] = CB
            for fc in range(NFC):
                gs, gv = wslab_load(gate_in[e, fc, :, :])
                us, uv = wslab_load(up_in[e, fc, :, :])
                for tb in range(NTB):
                    GM = mm_group(
                        bankt[tb][:, 0:TB],
                        [(wbuf[gs][:, dc * 128:(dc + 1) * 128], xns(dc, tb))
                         for dc in range(NDC)],
                        wait=[(f"WS{gs}", gv), ("PDVE", xn_done[tb])],
                        bankw=[tb])
                    UM = mm_group(
                        bankt[2 + tb][:, 0:TB],
                        [(wbuf[us][:, dc * 128:(dc + 1) * 128], xns(dc, tb))
                         for dc in range(NDC)],
                        wait=[(f"WS{us}", uv)], bankw=[2 + tb])
                    if tb == NTB - 1:
                        wring.consumed(wi[0] - 2, "PPE", GM)
                        wring.consumed(wi[0] - 1, "PPE", UM)
                    i0 = fci[0]
                    i1 = fci[0] + 1
                    fci[0] += 2
                    s0 = fsc_ring.slot(i0)
                    s1 = fsc_ring.slot(i1)
                    wl = [("PPE", GM)]
                    pw = fsc_ring.prod_wait(i0)
                    if pw:
                        wl.append(pw)
                    SL = act(lambda E_, t=tb, s=s0: E_.activation(
                        fsc[:, s * TB:(s + 1) * TB], bankt[t][:, 0:TB], AF.Silu),
                        wait=wl)
                    bank_rd[tb] = ("PACT", SL)
                    wt = [("PACT", SL), ("PPE", UM)]
                    pw = fsc_ring.prod_wait(i1)
                    if pw:
                        wt.append(pw)
                    T1 = dve(lambda E_, t=tb, a=s0, c=s1: E_.tensor_mul(
                        fsc[:, c * TB:(c + 1) * TB], fsc[:, a * TB:(a + 1) * TB],
                        bankt[2 + t][:, 0:TB]), wait=wt)
                    bank_rd[2 + tb] = ("PDVE", T1)
                    fsc_ring.consumed(i0, "PDVE", T1)
                    HH = dve(lambda E_, t=tb, c=s1, f=fc: E_.tensor_mul(
                        h_sb[:, (f * NTB + t) * TB:(f * NTB + t + 1) * TB],
                        fsc[:, c * TB:(c + 1) * TB], bankt[4 + t][:, 0:TB]),
                        wait=[("PPE", cbv[tb])])
                    fsc_ring.consumed(i1, "PDVE", HH)
                    if fc == NFC - 1:
                        bank_rd[4 + tb] = ("PDVE", HH)
            H_ALL = HH
            for oc in range(NDC):
                ds, dv = dslab_load(down_in[e, oc, :, :])
                for tb in range(NTB):
                    bi = 6 + (oc * NTB + tb) % 2
                    DM = mm_group(
                        bankt[bi][:, 0:TB],
                        [(dbuf[ds][:, fc * 128:(fc + 1) * 128],
                          h_sb[:, (fc * NTB + tb) * TB:(fc * NTB + tb + 1) * TB])
                         for fc in range(NFC)],
                        wait=[(f"DS{ds}", dv), ("PDVE", H_ALL)], bankw=[bi])
                    if tb == NTB - 1:
                        dring.consumed(di[0] - 1, "PPE", DM)
                    wx = [("PPE", DM)]
                    XADD = dve(lambda E_, d=oc, t=tb, i=bi: E_.tensor_add(
                        xs(d, t), xs(d, t), bankt[i][:, 0:TB]), wait=wx)
                    bank_rd[bi] = ("PDVE", XADD)
                    xadd_map[(oc, tb)] = XADD
        return xadd_map

    def qkv(xn_done):
        stg4 = Slots(4)
        sti = [0]
        CCK_v = CCV_v = None
        for oc in list(range(16, 48)) + list(range(16)):
            ws, wv = wslab_load(wqkv_in[oc, :, :])
            spfx = "KQ" if oc < 16 else ("KK" if oc < 32 else "KV")
            for tb in range(NTB):
                bi = (oc * NTB + tb) % 4
                QM = mm_group(
                    bankt[bi][:, 0:TB],
                    [(wbuf[ws][:, dc * 128:(dc + 1) * 128], xns(dc, tb))
                     for dc in range(NDC)],
                    wait=[(f"WS{ws}", wv), ("PDVE", xn_done[tb])], bankw=[bi])
                if tb == NTB - 1:
                    wring.consumed(wi[0] - 1, "PPE", QM)
                i = sti[0]; sti[0] += 1
                s = stg4.slot(i)
                sg = f"{spfx}{s}"
                pw = stg4.prod_wait(i)
                wl = [("PPE", QM)] + ([pw] if pw else [])
                if oc < 16:
                    EV = dve(lambda E_, i=bi, s=s, o=oc: E_.tensor_scalar(
                        stg[:, s * TB:(s + 1) * TB], bankt[i][:, 0:TB],
                        RSCALE, bqs_sb[:, o:o + 1], ALU.mult, ALU.add), wait=wl)
                else:
                    EV = dve(lambda E_, i=bi, s=s, o=oc: E_.tensor_scalar(
                        stg[:, s * TB:(s + 1) * TB], bankt[i][:, 0:TB],
                        bq_sb[:, o:o + 1], None, ALU.add), wait=wl)
                bank_rd[bi] = ("PDVE", EV)
                if oc < 16:
                    dst = q_dram[oc * 128:(oc + 1) * 128, tb * TB:(tb + 1) * TB]
                elif oc < 32:
                    r0 = (oc - 16) * 128
                    dst = kv_contrib[r0:r0 + 128, tb * TB:(tb + 1) * TB]
                else:
                    r0 = 2048 + (oc - 32) * 128
                    dst = kv_contrib[r0:r0 + 128, tb * TB:(tb + 1) * TB]
                sgv = sy_dma(sg, dst, stg[:, s * TB:(s + 1) * TB],
                             wait=[("PDVE", EV)])
                stg4.consumed(i, sg, sgv)
            if oc == 39:
                for s in range(4):
                    b.wait("gpsimd", f"KK{s}", b.cnt[f"KK{s}"])
                CCK_v = b.step("gpsimd", "CCK", lambda E_: E_.collective_compute(
                    "AllGather", ALU.bypass, [[0, 1, 2, 3], [4, 5, 6, 7]],
                    [kv_contrib[0:2048, :].opt()], [kvgK.ap().opt()]), 1)
            if oc == 3:
                for s in range(4):
                    b.wait("gpsimd", f"KV{s}", b.cnt[f"KV{s}"])
                CCV_v = b.step("gpsimd", "CCV", lambda E_: E_.collective_compute(
                    "AllGather", ALU.bypass, [[0, 1, 2, 3], [4, 5, 6, 7]],
                    [kv_contrib[2048:4096, :].opt()], [kvgV.ap().opt()]), 1)
        for s in range(4):
            b.wait("sync", f"KQ{s}", b.cnt[f"KQ{s}"])
        return CCK_v, CCV_v

    def attention(CCK_v, CCV_v, qkv_pe_done):
        kh_cnt = [0, 0]
        punits = Slots(4)
        head_pe_done = [None, None]
        sc_bank = [0, 1, 6]
        u_glob = [0]
        AT_DONE = None
        for n in range(NH):
            buf = n % 2
            wl = []
            if head_pe_done[buf] is not None:
                wl.append(("PPE", head_pe_done[buf]))
            sy_dma(f"KH{buf}", qbuf[buf][:, :],
                   q_dram[n * 128:(n + 1) * 128, :], wait=wl)
            b.wait("sync", "CCK", CCK_v)
            for r in range(4):
                sy_dma(f"KH{buf}", kbuf[buf][:, r * 1024:(r + 1) * 1024],
                       kvgK[2048 * r + 128 * n: 2048 * r + 128 * (n + 1), :])
            b.wait("sync", "CCV", CCV_v)
            for r in range(4):
                for kb in range(8):
                    sy_dma_t(f"KH{buf}",
                             vbuf[buf][:, (r * 8 + kb) * 128:(r * 8 + kb + 1) * 128],
                             kvgV[2048 * r + 128 * n: 2048 * r + 128 * (n + 1),
                                  kb * 128:(kb + 1) * 128])
            kh_cnt[buf] += 37 * 16
            KH_READY = kh_cnt[buf]

            for qt in range(NQT):
                units = [(kb, r) for kb in range(2 * qt + 2) for r in range(4)]
                nu = len(units)
                ob = 2 + qt % 2
                zc = (qt % 2) * NQ
                pend = []
                sc_of = {}

                def emit_sc(u):
                    kb, r = units[u]
                    sb = sc_bank[u % 3]
                    wl = [(f"KH{buf}", KH_READY)]
                    v = pe(lambda E_, sb=sb, kb=kb, r=r, bu=buf, q=qt: E_.matmul(
                        bankt[sb][:, 0:NQ],
                        kbuf[bu][:, (r * 8 + kb) * 128:(r * 8 + kb + 1) * 128],
                        qbuf[bu][:, q * NQ:(q + 1) * NQ],
                        start=True, stop=True),
                        wait=wl, bankw=[sb])
                    sc_of[u] = (sb, v)

                def emit_exp_mask(u):
                    kb, r = units[u]
                    sb, scv = sc_of[u]
                    ps = punits.slot(u_glob[0] + u)
                    wl = [("PPE", scv)]
                    pw = punits.prod_wait(u_glob[0] + u)
                    if pw:
                        wl.append(pw)
                    EX = act(lambda E_, sb=sb, ps=ps: E_.activation(
                        ppool[:, ps * NQ:(ps + 1) * NQ], bankt[sb][:, 0:NQ],
                        AF.Exp), wait=wl)
                    bank_rd[sb] = ("PACT", EX)
                    dep = ("PACT", EX)
                    if kb >= 2 * qt:
                        d = kb - 2 * qt
                        MK = dve(lambda E_, ps=ps, r=r, d=d: E_.tensor_mul(
                            ppool[:, ps * NQ:(ps + 1) * NQ],
                            ppool[:, ps * NQ:(ps + 1) * NQ],
                            mask_sb[:, (r * 2 + d) * NQ:(r * 2 + d + 1) * NQ]),
                            wait=[("PACT", EX)])
                        dep = ("PDVE", MK)
                    return dep

                def emit_avz(u, dep):
                    kb, r = units[u]
                    ps = punits.slot(u_glob[0] + u)
                    st = (u == 0)
                    sp = (u == nu - 1)
                    bw = [ob] if st else []
                    b.wait("tensor", *dep)
                    if st:
                        bank_wait("tensor", ob)
                        bank_wait("tensor", 4)
                    b.emit("tensor", lambda E_, ps=ps, kb=kb, r=r, st=st, sp=sp,
                           o=ob, bu=buf:
                           E_.matmul(bankt[o][:, 0:NQ],
                                     vbuf[bu][:, (r * 8 + kb) * 128:
                                              (r * 8 + kb + 1) * 128],
                                     ppool[:, ps * NQ:(ps + 1) * NQ],
                                     start=st, stop=sp))
                    v = b.step("tensor", "PPE",
                               lambda E_, ps=ps, st=st, sp=sp, z=zc: E_.matmul(
                                   bankt[4][0:1, z:z + NQ], ones_cb[:, :],
                                   ppool[:, ps * NQ:(ps + 1) * NQ],
                                   start=st, stop=sp))
                    punits.consumed(u_glob[0] + u, "PPE", v)
                    return v

                deps = {}
                for u in range(nu + 2):
                    if u < nu:
                        emit_sc(u)
                        deps[u] = emit_exp_mask(u)
                    if u >= 2:
                        AVZ = emit_avz(u - 2, deps.pop(u - 2))
                u_glob[0] += nu
                RZ = dve(lambda E_, z=zc: E_.reciprocal(R_rz[0:1, z:z + NQ],
                                                        bankt[4][0:1, z:z + NQ]),
                         wait=[("PPE", AVZ)])
                ZB = pe(lambda E_, z=zc: E_.matmul(bankt[5][:, z:z + NQ],
                                                   onesf[:, 0:128],
                                                   R_rz[:, z:z + NQ],
                                                   start=True, stop=True),
                        wait=[("PDVE", RZ)], bankw=[5])
                ZBC = dve(lambda E_, z=zc: E_.tensor_copy(zbs[:, z:z + NQ],
                                                          bankt[5][:, z:z + NQ]),
                          wait=[("PPE", ZB)])
                bank_rd[5] = ("PDVE", ZBC)
                wl = [("PDVE", ZBC)]
                if qkv_pe_done is not None:
                    wl.append(("PPE", qkv_pe_done))
                AT = dve(lambda E_, n=n, q=qt, o=ob, z=zc: E_.tensor_mul(
                    xn[:, n * T + q * NQ: n * T + (q + 1) * NQ],
                    bankt[o][:, 0:NQ], zbs[:, z:z + NQ]), wait=wl)
                bank_rd[ob] = ("PDVE", AT)
                bank_rd[4] = ("PDVE", RZ)
                AT_DONE = AT
            head_pe_done[buf] = b.cnt["PPE"]
        return AT_DONE

    def out_proj(at_done, ffn1_xadd):
        xadd_map = {}
        for oc in range(NDC):
            ws, wv = wslab_load(wo_in[oc, :, :])
            for tb in range(NTB):
                bi = (oc * NTB + tb) % 4
                OM = mm_group(
                    bankt[bi][:, 0:TB],
                    [(wbuf[ws][:, dc * 128:(dc + 1) * 128], xns(dc, tb))
                     for dc in range(NDC)],
                    wait=[(f"WS{ws}", wv), ("PDVE", at_done)], bankw=[bi])
                if tb == NTB - 1:
                    wring.consumed(wi[0] - 1, "PPE", OM)
                XADD = dve(lambda E_, i=bi, d=oc, t=tb: E_.scalar_tensor_tensor(
                    xs(d, t), bankt[i][:, 0:TB], bo_sb[:, d:d + 1], xs(d, t),
                    ALU.add, ALU.add), wait=[("PPE", OM)])
                bank_rd[bi] = ("PDVE", XADD)
                xadd_map[(oc, tb)] = XADD
        return xadd_map

    xn1 = ln(xdep_dve=None, want_router=True, call_idx=0)
    F1 = ffn(0, xn1)
    xn2 = ln(xdep_map=F1)
    CCK_v, CCV_v = qkv(xn2)
    QKV_PE = b.cnt["PPE"]
    ATD = attention(CCK_v, CCV_v, QKV_PE)
    OPm = out_proj(ATD, F1)
    xn3 = ln(xdep_map=OPm, want_router=True, call_idx=1)
    F2 = ffn(1, xn3)

    for i in range(4):
        dep = max(F2[(oc, tb)] for oc in range(4 * i, 4 * i + 4) for tb in range(NTB))
        sy_dma("DO", out_t[:, i * 4096:(i + 1) * 4096],
               x_sb[:, i * 4096:(i + 1) * 4096], wait=[("PDVE", dep)])
    b.wait("sync", "DO", b.cnt["DO"])

    with nc.Block() as block:
        @block.tensor
        def _(E):
            b.run("tensor", E)

        @block.scalar
        def _(E):
            b.run("scalar", E)

        @block.vector
        def _(E):
            b.run("vector", E)

        @block.gpsimd
        def _(E):
            b.run("gpsimd", E)

        @block.sync
        def _(E):
            b.run("sync", E)

    return nc



def _prep_shared(Wqkv, Wo, gate_w, up_w, down_w, router_w, bqkv, bo):
    f32 = np.float32
    wqkv = np.ascontiguousarray(
        np.asarray(Wqkv, f32).reshape(16, 128, 48, 128).transpose(2, 1, 0, 3)
    ).reshape(48, 128, 2048)
    wo = np.ascontiguousarray(
        np.asarray(Wo, f32).reshape(16, 128, 16, 128).transpose(2, 1, 0, 3)
    ).reshape(16, 128, 2048)
    gate = np.ascontiguousarray(
        np.asarray(gate_w, f32).reshape(16, 16, 128, 6, 128).transpose(0, 3, 2, 1, 4)
    ).reshape(16, 6, 128, 2048)
    up = np.ascontiguousarray(
        np.asarray(up_w, f32).reshape(16, 16, 128, 6, 128).transpose(0, 3, 2, 1, 4)
    ).reshape(16, 6, 128, 2048)
    down = np.ascontiguousarray(
        np.asarray(down_w, f32).reshape(16, 6, 128, 16, 128).transpose(0, 3, 2, 1, 4)
    ).reshape(16, 16, 128, 768)
    router = np.ascontiguousarray(
        np.asarray(router_w, f32).reshape(16, 128, 16).transpose(1, 0, 2)
    ).reshape(128, 256)
    bq = np.ascontiguousarray(np.asarray(bqkv, f32).reshape(48, 128).T)
    bo_t = np.ascontiguousarray(np.asarray(bo, f32).reshape(16, 128).T)
    ident = np.eye(128, dtype=f32)
    return wqkv, wo, gate, up, down, router, bq, bo_t, ident


def _masks_for_core(c):
    m = np.zeros((128, 4, 2, 256), dtype=np.float32)
    s = np.arange(128)[:, None]
    j = np.arange(256)[None, :]
    for r in range(4):
        off = 0 if c >= r else 1
        for d in range(2):
            m[:, r, d, :] = (j >= 128 * d + s + off).astype(np.float32)
    return m.reshape(128, 2048).astype(ml_dtypes.bfloat16)


def kernel(L0_attn_x, L0_attn_Wqkv, L0_attn_bqkv, L0_attn_Wo, L0_attn_bo,
           L0_ffn_router_w, L0_ffn_gate_w, L0_ffn_up_w, L0_ffn_down_w):
    x = np.asarray(L0_attn_x, np.float32)
    wqkv, wo, gate, up, down, router, bq, bo_t, ident = _prep_shared(
        L0_attn_Wqkv, L0_attn_Wo, L0_ffn_gate_w, L0_ffn_up_w, L0_ffn_down_w,
        L0_ffn_router_w, L0_attn_bqkv, L0_attn_bo)

    nc = build()
    in_maps = []
    for core in range(8):
        g, c = core // 4, core % 4
        xs_ = x[g, c::4, :]
        x_fm = np.ascontiguousarray(
            xs_.T.reshape(16, 128, T).transpose(1, 0, 2)).reshape(128, NDC * T)
        in_maps.append({
            "x": x_fm, "wqkv": wqkv, "wo": wo, "gate": gate, "up": up,
            "down": down, "router": router, "bqkv": bq, "bo": bo_t,
            "identw": ident, "masks": _masks_for_core(c),
        })
    res = run_bass_kernel_spmd(nc, in_maps, core_ids=list(range(8)))
    out = np.empty((2, 4096, D), dtype=np.float32)
    for core in range(8):
        g, c = core // 4, core % 4
        o = np.asarray(res.results[core]["out"])
        fm = o.reshape(128, NDC, T).transpose(1, 0, 2).reshape(D, T)
        out[g, c::4, :] = fm.T
    return out


if __name__ == "__main__":
    import time
    t0 = time.time()
    nc = build()
    print(f"built in {time.time()-t0:.1f}s, n_instructions={nc.n_instructions()}")


# version 07 (final)
# speedup vs baseline: 1.0442x; 1.0442x over previous
import math
import numpy as np
import ml_dtypes
import concourse.bass as bass
import concourse.mybir as mybir
from concourse.bass_utils import run_bass_kernel_spmd

F32 = mybir.dt.float32
BF16 = mybir.dt.bfloat16
AF = mybir.ActivationFunctionType
ALU = mybir.AluOpType
AX = mybir.AxisListType

D = 2048
NDC = 16
T = 1024
TB = 512
NTB = 2
E = 16
NFC = 6
NH = 16
NQT = 4
NQ = 256
RSCALE = 1.0 / math.sqrt(128.0)
EPS = 1e-5

ENGS = ("tensor", "scalar", "vector", "gpsimd", "sync")


class B:

    def __init__(self, nc):
        self.nc = nc
        self.q = {e: [] for e in ENGS}
        self.sem = {}
        self.cnt = {}
        self.waited = {e: {} for e in ENGS}

    def mksem(self, name):
        self.sem[name] = self.nc.alloc_semaphore(name)
        self.cnt[name] = 0

    def emit(self, eng, fn):
        self.q[eng].append(fn)

    def wait(self, eng, sname, val):
        if val is None or val <= 0:
            return
        assert val <= self.cnt[sname], (
            f"{eng} waits {sname}>={val} but only {self.cnt[sname]} emitted")
        if self.waited[eng].get(sname, 0) >= val:
            return
        self.waited[eng][sname] = val
        s = self.sem[sname]
        self.emit(eng, lambda E, s=s, v=val: E.wait_ge(s, v))

    SELF_SYNC = {"vector": "PDVE", "scalar": "PACT"}

    def step(self, eng, sname, fn, inc=1):
        own = self.SELF_SYNC.get(eng)
        if own is not None and self.cnt[own] > 0:
            self.wait(eng, own, self.cnt[own])
        self.cnt[sname] += inc
        s = self.sem[sname]
        self.emit(eng, lambda E, fn=fn, s=s, inc=inc: fn(E).then_inc(s, inc))
        return self.cnt[sname]

    def run(self, eng_name, E):
        for fn in self.q[eng_name]:
            fn(E)


class Slots:

    def __init__(self, n):
        self.n = n
        self.free = [None] * n

    def slot(self, i):
        return i % self.n

    def prod_wait(self, i):
        return self.free[i % self.n]

    def consumed(self, i, sname, val):
        self.free[i % self.n] = (sname, val)


def build():
    nc = bass.Bass("TRN2", target_bir_lowering=False)

    x_in = nc.dram_tensor("x", [128, NDC * T], F32, kind="ExternalInput")
    wqkv_in = nc.dram_tensor("wqkv", [48, 128, 2048], F32, kind="ExternalInput")
    wo_in = nc.dram_tensor("wo", [16, 128, 2048], F32, kind="ExternalInput")
    gate_in = nc.dram_tensor("gate", [16, 6, 128, 2048], F32, kind="ExternalInput")
    up_in = nc.dram_tensor("up", [16, 6, 128, 2048], F32, kind="ExternalInput")
    down_in = nc.dram_tensor("down", [16, 16, 128, 768], F32, kind="ExternalInput")
    router_in = nc.dram_tensor("router", [128, 256], F32, kind="ExternalInput")
    bqkv_in = nc.dram_tensor("bqkv", [128, 48], F32, kind="ExternalInput")
    bo_in = nc.dram_tensor("bo", [128, 16], F32, kind="ExternalInput")
    ident_in = nc.dram_tensor("identw", [128, 128], F32, kind="ExternalInput")
    masks_in = nc.dram_tensor("masks", [128, 2048], BF16, kind="ExternalInput")
    out_t = nc.dram_tensor("out", [128, NDC * T], F32, kind="ExternalOutput")
    kv_contrib = nc.dram_tensor("kvc", [4096, T], BF16, kind="Internal")
    kvgK = nc.dram_tensor("kvgK", [4 * 2048, T], BF16, kind="Internal")
    kvgV = nc.dram_tensor("kvgV", [4 * 2048, T], BF16, kind="Internal")
    q_dram = nc.dram_tensor("qd", [2048, T], BF16, kind="Internal")
    ct_dram = nc.dram_tensor("ctd", [2, 16, T], F32, kind="Internal")

    A = nc.alloc_sbuf_tensor
    x_sb = A("x_sb", [128, NDC * T], F32)
    xn = A("xn", [128, NDC * T], BF16)
    h_sb = A("h_sb", [128, NFC * NTB * TB], BF16)
    wbuf = [A(f"wbuf{i}", [128, 2048], BF16) for i in range(3)]
    dbuf = [A(f"dbuf{i}", [128, 768], BF16) for i in range(4)]
    kbuf = [A(f"kbuf{i}", [128, 4096], BF16) for i in range(2)]
    vbuf = [A(f"vbuf{i}", [128, 4096], BF16) for i in range(2)]
    qbuf = [A(f"qbuf{i}", [128, 1024], BF16) for i in range(2)]
    ppool = A("ppool", [128, 6 * NQ], BF16)
    mask_sb = A("mask_sb", [128, 2048], BF16)
    xbp = A("xbp", [128, 2 * TB], BF16)
    xqp = A("xqp", [128, 2 * TB], BF16)
    fsc = A("fsc", [128, 3 * TB], F32)
    stg = A("stg", [128, 4 * TB], BF16)
    zbs = A("zbs", [128, NQ], F32)
    router_sb = A("router_sb", [128, 512], F32)
    R_mr = A("R_mr", [128, 4 * TB], BF16)
    R_ce = A("R_ce", [128, 2 * TB], BF16)
    R_rz = A("R_rz", [128, 2 * NQ], F32)
    smallp = A("smallp", [128, 256], F32)
    ct_sb = A("ct_sb", [16, T], F32)
    onesf = A("onesf", [128, 256], F32)
    onesb = A("onesb", [128, 128], BF16)
    ones_cb = A("ones_cb", [128, 1], BF16)
    ones_cf = A("ones_cf", [128, 1], F32)
    rowp = A("rowp", [128, 1536], F32)
    colsp = A("colsp", [128, 16], F32)
    bq_sb = A("bq_sb", [128, 48], F32)
    bqs_sb = A("bqs_sb", [128, 16], F32)
    bo_sb = A("bo_sb", [128, 16], F32)

    bankt = [nc.alloc_psum_tensor(f"bank{i}", [128, 512], F32) for i in range(8)]

    b = B(nc)
    for s in ("PPE", "PACT", "PDVE", "DIN",
              "WS0", "WS1", "WS2", "DS0", "DS1", "DS2", "DS3",
              "CE0", "CE1", "SG0", "SG1", "SG2", "SG3", "CT",
              "KQ0", "KQ1", "KQ2", "KQ3", "KK0", "KK1", "KK2", "KK3",
              "KV0", "KV1", "KV2", "KV3", "DX0", "DX1", "DX2", "DX3",
              "KH0", "KH1", "CCK", "CCV", "DO"):
        b.mksem(s)

    bank_rd = [None] * 8

    def bank_wait(eng, i):
        if bank_rd[i] is not None:
            b.wait(eng, *bank_rd[i])

    def pe(fn, wait=(), bankw=()):
        for w in wait:
            b.wait("tensor", *w)
        for i in bankw:
            bank_wait("tensor", i)
        return b.step("tensor", "PPE", fn)

    def act(fn, wait=(), bankw=()):
        for w in wait:
            b.wait("scalar", *w)
        for i in bankw:
            bank_wait("scalar", i)
        return b.step("scalar", "PACT", fn)

    def dve(fn, wait=(), bankw=()):
        for w in wait:
            b.wait("vector", *w)
        for i in bankw:
            bank_wait("vector", i)
        return b.step("vector", "PDVE", fn)

    def mm_group(psum_ap, ops, wait=(), bankw=()):
        for w in wait:
            b.wait("tensor", *w)
        for i in bankw:
            bank_wait("tensor", i)
        n = len(ops)
        v = None
        for j, (l, r) in enumerate(ops):
            fn = (lambda E, o=psum_ap, l=l, r=r, st=(j == 0), sp=(j == n - 1):
                  E.matmul(o, l, r, start=st, stop=sp))
            if j == n - 1:
                v = b.step("tensor", "PPE", fn)
            else:
                b.emit("tensor", fn)
        return v

    def sy_dma(sname, dst, src, wait=()):
        for w in wait:
            b.wait("sync", *w)
        return b.step("sync", sname, lambda E, d=dst, s=src: E.dma_start(d, s), 16)

    def sc_dma(sname, dst, src, wait=(), transpose=False):
        for w in wait:
            b.wait("scalar", *w)
        return b.step("scalar", sname,
                      lambda E, d=dst, s=src, t=transpose: E.dma_start(d, s, transpose=t),
                      16)

    def sy_dma_t(sname, dst, src, wait=()):
        for w in wait:
            b.wait("sync", *w)
        return b.step("sync", sname,
                      lambda E, d=dst, s=src: E.dma_start(d, s, transpose=True), 16)

    def gp_dma(sname, dst, src, wait=()):
        for w in wait:
            b.wait("gpsimd", *w)
        return b.step("gpsimd", sname, lambda E, d=dst, s=src: E.dma_start(d, s), 16)

    def xs(dc, tb):
        return x_sb[:, dc * T + tb * TB: dc * T + (tb + 1) * TB]

    def xns(dc, tb):
        return xn[:, dc * T + tb * TB: dc * T + (tb + 1) * TB]

    for j in range(4):
        sy_dma(f"DX{j}", x_sb[:, j * 4096:(j + 1) * 4096],
               x_in[:, j * 4096:(j + 1) * 4096])
    sy_dma("DIN", router_sb[:, 0:256], router_in[:, :])
    sy_dma("DIN", bq_sb[:, :], bqkv_in[:, :])
    sy_dma("DIN", bo_sb[:, :], bo_in[:, :])
    sy_dma("DIN", onesf[:, 128:256], ident_in[:, :])
    sy_dma("DIN", mask_sb[:, :], masks_in[:, :])
    DIN_ALL = b.cnt["DIN"]

    for fn in (
        lambda E: E.memset(onesf[:, 0:128], 1.0),
        lambda E: E.memset(onesb[:, :], 1.0),
        lambda E: E.memset(ones_cb[:, :], 1.0),
        lambda E: E.memset(ones_cf[:, :], 1.0),
        lambda E: E.memset(R_mr[:, :], 0.0),
        lambda E: E.memset(R_ce[:, :], 0.0),
        lambda E: E.memset(R_rz[:, :], 0.0),
    ):
        dve(fn)
    INIT = dve(lambda E: E.memset(smallp[:, :], 0.0))
    BQS = dve(lambda E: E.tensor_scalar(bqs_sb[:, :], bq_sb[:, 0:16], RSCALE, None,
                                        ALU.mult),
              wait=[("DIN", DIN_ALL)])

    CSR = mm_group(bankt[7][0:1, 0:16],
                   [(ones_cf[:, :], router_sb[:, dc * 16:dc * 16 + 16])
                    for dc in range(NDC)],
                   wait=[("DIN", DIN_ALL), ("PDVE", INIT)])
    CSRROW = dve(lambda E: E.tensor_copy(smallp[0:1, 0:16], bankt[7][0:1, 0:16]),
                 wait=[("PPE", CSR)])
    bank_rd[7] = ("PDVE", CSRROW)
    CSRB_MM = pe(lambda E: E.matmul(bankt[6][:, 0:16], onesf[:, 0:128],
                                    smallp[:, 0:16], start=True, stop=True),
                 wait=[("PDVE", CSRROW)])
    CSRB = dve(lambda E: E.tensor_copy(smallp[:, 32:48], bankt[6][:, 0:16]),
               wait=[("PPE", CSRB_MM)])
    bank_rd[6] = ("PDVE", CSRB)
    csrb = smallp[:, 32:48]

    xb_ring = Slots(2)
    xq_ring = Slots(2)
    fsc_ring = Slots(3)
    stg_ring = Slots(3)
    fci = [0]
    ln_count = [0]

    def ln(xdep_dve=None, xdep_map=None, want_router=False, call_idx=0):
        li = ln_count[0]
        ln_count[0] += 1

        def xdep(dc, tb):
            if xdep_map is not None:
                return xdep_map[(dc, tb)]
            return xdep_dve
        xn_done = [None, None]
        mr_rows = {}
        for tb in range(NTB):
            for dc in range(NDC):
                i = tb * NDC + dc
                s = xb_ring.slot(i)
                w = [(f"DX{dc // 4}", 16)]
                if xdep(dc, tb):
                    w.append(("PDVE", xdep(dc, tb)))
                pw = xb_ring.prod_wait(i)
                if pw:
                    w.append(pw)
                xbv = dve(lambda E, d=dc, t=tb, s=s: E.tensor_copy(
                    xbp[:, s * TB:(s + 1) * TB], xs(d, t)), wait=w)
                wq = [(f"DX{dc // 4}", 16)]
                if xdep(dc, tb):
                    wq.append(("PDVE", xdep(dc, tb)))
                pwq = xq_ring.prod_wait(i)
                if pwq:
                    wq.append(pwq)
                xqv = act(lambda E, d=dc, t=tb, s=s: E.activation(
                    xqp[:, s * TB:(s + 1) * TB], xs(d, t), AF.Square), wait=wq)
                bw = [0] if dc == 0 else []
                v1 = pe(lambda E, s=s, st=(dc == 0), sp=(dc == NDC - 1):
                        E.matmul(bankt[0][0:1, 0:TB], ones_cb[:, :],
                                 xbp[:, s * TB:(s + 1) * TB], start=st, stop=sp),
                        wait=[("PDVE", xbv)], bankw=bw)
                v2 = pe(lambda E, s=s, st=(dc == 0), sp=(dc == NDC - 1):
                        E.matmul(bankt[1][0:1, 0:TB], ones_cb[:, :],
                                 xqp[:, s * TB:(s + 1) * TB], start=st, stop=sp),
                        wait=[("PACT", xqv)], bankw=[1] if dc == 0 else [])
                xb_ring.consumed(i, "PPE", v1)
                xq_ring.consumed(i, "PPE", v2)
            S_MM = v2
            MF = dve(lambda E: E.tensor_scalar(rowp[0:1, 0:TB], bankt[0][0:1, 0:TB],
                                               1.0 / D, None, ALU.mult),
                     wait=[("PPE", S_MM)])
            bank_rd[0] = ("PDVE", MF)
            MROW = dve(lambda E, t=tb: E.tensor_copy(R_mr[0:1, t * TB:(t + 1) * TB],
                                                     rowp[0:1, 0:TB]))
            S2S = dve(lambda E: E.tensor_scalar(rowp[0:1, TB:2 * TB],
                                                bankt[1][0:1, 0:TB],
                                                1.0 / D, None, ALU.mult))
            bank_rd[1] = ("PDVE", S2S)
            M2 = dve(lambda E: E.tensor_mul(rowp[0:1, 2 * TB:3 * TB],
                                            rowp[0:1, 0:TB], rowp[0:1, 0:TB]))
            VV = dve(lambda E: E.tensor_sub(rowp[0:1, TB:2 * TB],
                                            rowp[0:1, TB:2 * TB],
                                            rowp[0:1, 2 * TB:3 * TB]))
            VE = dve(lambda E: E.tensor_scalar(rowp[0:1, TB:2 * TB],
                                               rowp[0:1, TB:2 * TB], EPS, None,
                                               ALU.add))
            STD = act(lambda E: E.activation(rowp[0:1, 0:TB],
                                             rowp[0:1, TB:2 * TB], AF.Sqrt),
                      wait=[("PDVE", VE)])
            RI = dve(lambda E: E.reciprocal(rowp[0:1, 2 * TB:3 * TB],
                                            rowp[0:1, 0:TB]),
                     wait=[("PACT", STD)])
            RROW = dve(lambda E, t=tb: E.tensor_copy(
                R_mr[0:1, (2 + t) * TB:(3 + t) * TB], rowp[0:1, 2 * TB:3 * TB]))
            mr_rows[tb] = (MROW, RROW)
            MB = pe(lambda E, t=tb: E.matmul(bankt[2][:, 0:TB], onesb[:, :],
                                             R_mr[:, t * TB:(t + 1) * TB],
                                             start=True, stop=True),
                    wait=[("PDVE", MROW)], bankw=[2])
            RB = pe(lambda E, t=tb: E.matmul(bankt[3][:, 0:TB], onesb[:, :],
                                             R_mr[:, (2 + t) * TB:(3 + t) * TB],
                                             start=True, stop=True),
                    wait=[("PDVE", RROW)], bankw=[3])
            for dc in range(NDC):
                i = fci[0]
                fci[0] += 1
                s = fsc_ring.slot(i)
                w = [("PPE", MB), (f"DX{dc // 4}", 16)]
                pw = fsc_ring.prod_wait(i)
                if pw:
                    w.append(pw)
                TT = dve(lambda E, d=dc, t=tb, s=s: E.tensor_sub(
                    fsc[:, s * TB:(s + 1) * TB], xs(d, t), bankt[2][:, 0:TB]),
                    wait=w)
                XN = dve(lambda E, d=dc, t=tb, s=s: E.tensor_mul(
                    xns(d, t), fsc[:, s * TB:(s + 1) * TB], bankt[3][:, 0:TB]),
                    wait=[("PPE", RB)])
                fsc_ring.consumed(i, "PDVE", XN)
            xn_done[tb] = XN
            bank_rd[2] = ("PDVE", XN)
            bank_rd[3] = ("PDVE", XN)

        if want_router:
            xa = (max(xdep_map.values()) if xdep_map is not None else xdep_dve)
            router(call_idx, xa, mr_rows)
        return xn_done

    def router(call_idx, xdep_all, mr_rows):
        rt = [router_sb[:, 256 + k * 16: 256 + (k + 1) * 16] for k in range(4)]
        for tt in range(8):
            tb = tt // 4
            rb_i = 4 + (tt % 2)
            w = [("DIN", DIN_ALL)] + [(f"DX{j}", 16) for j in range(4)]
            if xdep_all:
                w.append(("PDVE", xdep_all))
            RAW = mm_group(
                bankt[rb_i][:, 0:16],
                [(x_sb[:, dc * T + tt * 128: dc * T + (tt + 1) * 128],
                  router_sb[:, dc * 16:dc * 16 + 16]) for dc in range(NDC)],
                wait=w, bankw=[rb_i])
            MROW, RROW = mr_rows[tb]
            MC = pe(lambda E, t=tb, q=tt % 4: E.matmul(
                bankt[6][:, 0:1],
                R_mr[:, t * TB + q * 128: t * TB + (q + 1) * 128],
                onesb[:, 0:1], start=True, stop=True),
                wait=[("PDVE", MROW)], bankw=[6])
            RC = pe(lambda E, t=tb, q=tt % 4: E.matmul(
                bankt[6][:, 1:2],
                R_mr[:, (2 + t) * TB + q * 128: (2 + t) * TB + (q + 1) * 128],
                onesb[:, 0:1], start=True, stop=True),
                wait=[("PDVE", RROW)])
            MS = dve(lambda E: E.tensor_scalar(rt[0], csrb, bankt[6][0:128, 0:1],
                                               None, ALU.mult),
                     wait=[("PPE", MC)])
            S0 = dve(lambda E, i=rb_i: E.scalar_tensor_tensor(
                rt[1], bankt[i][:, 0:16], 1.0, rt[0], ALU.mult, ALU.subtract),
                wait=[("PPE", RAW)])
            bank_rd[rb_i] = ("PDVE", S0)
            S1 = dve(lambda E: E.tensor_scalar(rt[1], rt[1], bankt[6][0:128, 1:2],
                                               None, ALU.mult),
                     wait=[("PPE", RC)])
            bank_rd[6] = ("PDVE", S1)
            M1 = dve(lambda E: E.reduce_max(colsp[:, 0:1], rt[1], axis=AX.X))
            MK1 = dve(lambda E: E.tensor_scalar(rt[2], rt[1], colsp[:, 0:1], None,
                                                ALU.is_equal))
            TM = dve(lambda E: E.tensor_scalar(rt[3], rt[2], -1e30, None, ALU.mult))
            TM2 = dve(lambda E: E.tensor_add(rt[3], rt[3], rt[1]))
            M2_ = dve(lambda E: E.reduce_max(colsp[:, 1:2], rt[3], axis=AX.X))
            MK2 = dve(lambda E: E.tensor_scalar(rt[3], rt[1], colsp[:, 1:2], None,
                                                ALU.is_equal))
            DL = dve(lambda E: E.tensor_sub(colsp[:, 2:3], colsp[:, 0:1],
                                            colsp[:, 1:2]))
            W1 = act(lambda E: E.activation(colsp[:, 3:4], colsp[:, 2:3],
                                            AF.Sigmoid),
                     wait=[("PDVE", DL)])
            W1H = dve(lambda E: E.tensor_scalar(colsp[:, 4:5], colsp[:, 3:4], 0.5,
                                                None, ALU.mult),
                      wait=[("PACT", W1)])
            W2H = dve(lambda E: E.tensor_scalar(colsp[:, 5:6], colsp[:, 3:4], -0.5,
                                                0.5, ALU.mult, ALU.add))
            C1 = dve(lambda E: E.tensor_scalar(rt[2], rt[2], colsp[:, 4:5], None,
                                               ALU.mult))
            C2 = dve(lambda E: E.tensor_scalar(rt[3], rt[3], colsp[:, 5:6], None,
                                               ALU.mult))
            CC_ = dve(lambda E: E.tensor_add(rt[2], rt[2], rt[3]))
            TR = pe(lambda E: E.transpose(bankt[7][0:16, 0:128], rt[2],
                                          onesf[:, 128:256]),
                    wait=[("PDVE", CC_)], bankw=[7])
            ctw = [("PPE", TR)]
            if call_idx > 0:
                ctw.append(("CT", 16 * call_idx))
            CT_EV = dve(lambda E, q=tt: E.tensor_copy(ct_sb[:, q * 128:(q + 1) * 128],
                                                      bankt[7][0:16, 0:128]),
                        wait=ctw)
            bank_rd[7] = ("PDVE", CT_EV)
        sy_dma("CT", ct_dram[call_idx, :, :], ct_sb[:, :],
               wait=[("PDVE", CT_EV)])

    wring = Slots(3)
    dring = Slots(4)
    ce_ring = Slots(2)
    wi = [0]
    di = [0]
    cei = [0]

    def wslab_load(src_ap):
        i = wi[0]
        wi[0] += 1
        s = wring.slot(i)
        pw = wring.prod_wait(i)
        v = gp_dma(f"WS{s}", wbuf[s][:, :], src_ap, wait=[pw] if pw else [])
        return s, v

    def dslab_load(src_ap):
        i = di[0]
        di[0] += 1
        s = dring.slot(i)
        pw = dring.prod_wait(i)
        v = gp_dma(f"DS{s}", dbuf[s][:, 0:768], src_ap, wait=[pw] if pw else [])
        return s, v

    def ffn(call_idx, xn_done, xprev_pe=None):
        ct_need = 16 * (call_idx + 1)
        xadd_map = {}
        for e in range(E):
            ce_slots = {}
            for tb in range(NTB):
                i = cei[0]
                cei[0] += 1
                s = ce_ring.slot(i)
                pw = ce_ring.prod_wait(i)
                w = [("CT", ct_need)] + ([pw] if pw else [])
                cev = gp_dma(f"CE{s}", R_ce[0:1, s * TB:(s + 1) * TB],
                             ct_dram[call_idx, e, tb * TB:(tb + 1) * TB], wait=w)
                ce_slots[tb] = (i, s, cev)
            cbv = {}
            deferred = []
            for fc in range(NFC):
                gs, gv = wslab_load(gate_in[e, fc, :, :])
                us, uv = wslab_load(up_in[e, fc, :, :])
                gmum = {}
                for tb in range(NTB):
                    GM = mm_group(
                        bankt[tb][:, 0:TB],
                        [(wbuf[gs][:, dc * 128:(dc + 1) * 128], xns(dc, tb))
                         for dc in range(NDC)],
                        wait=[(f"WS{gs}", gv), ("PDVE", xn_done[tb])],
                        bankw=[tb])
                    UM = mm_group(
                        bankt[2 + tb][:, 0:TB],
                        [(wbuf[us][:, dc * 128:(dc + 1) * 128], xns(dc, tb))
                         for dc in range(NDC)],
                        wait=[(f"WS{us}", uv)], bankw=[2 + tb])
                    if tb == NTB - 1:
                        wring.consumed(wi[0] - 2, "PPE", GM)
                        wring.consumed(wi[0] - 1, "PPE", UM)
                    gmum[tb] = (GM, UM)
                if fc == 0:
                    for tb in range(NTB):
                        i, s, cev = ce_slots[tb]
                        CB = pe(lambda E_, s=s, t=tb: E_.matmul(
                            bankt[4 + t][:, 0:TB], onesb[:, :],
                            R_ce[:, s * TB:(s + 1) * TB], start=True, stop=True),
                            wait=[(f"CE{s}", cev)], bankw=[4 + tb])
                        ce_ring.consumed(i, "PPE", CB)
                        cbv[tb] = CB
                for tb in range(NTB):
                    GM, UM = gmum[tb]
                    i0 = fci[0]
                    i1 = fci[0] + 1
                    fci[0] += 2
                    s0 = fsc_ring.slot(i0)
                    s1 = fsc_ring.slot(i1)
                    wl = [("PPE", GM)]
                    pw = fsc_ring.prod_wait(i0)
                    if pw:
                        wl.append(pw)
                    SL = act(lambda E_, t=tb, s=s0: E_.activation(
                        fsc[:, s * TB:(s + 1) * TB], bankt[t][:, 0:TB], AF.Silu),
                        wait=wl)
                    bank_rd[tb] = ("PACT", SL)
                    wt = [("PACT", SL), ("PPE", UM)]
                    pw = fsc_ring.prod_wait(i1)
                    if pw:
                        wt.append(pw)
                    T1 = dve(lambda E_, t=tb, a=s0, c=s1: E_.tensor_mul(
                        fsc[:, c * TB:(c + 1) * TB], fsc[:, a * TB:(a + 1) * TB],
                        bankt[2 + t][:, 0:TB]), wait=wt)
                    bank_rd[2 + tb] = ("PDVE", T1)
                    fsc_ring.consumed(i0, "PDVE", T1)
                    HH = dve(lambda E_, t=tb, c=s1, f=fc: E_.tensor_mul(
                        h_sb[:, (f * NTB + t) * TB:(f * NTB + t + 1) * TB],
                        fsc[:, c * TB:(c + 1) * TB], bankt[4 + t][:, 0:TB]),
                        wait=[("PPE", cbv[tb])])
                    fsc_ring.consumed(i1, "PDVE", HH)
                    if fc == NFC - 1:
                        bank_rd[4 + tb] = ("PDVE", HH)
            H_ALL = HH
            for oc in range(NDC):
                ds, dv = dslab_load(down_in[e, oc, :, :])
                for tb in range(NTB):
                    bi = 6 + (oc * NTB + tb) % 2
                    DM = mm_group(
                        bankt[bi][:, 0:TB],
                        [(dbuf[ds][:, fc * 128:(fc + 1) * 128],
                          h_sb[:, (fc * NTB + tb) * TB:(fc * NTB + tb + 1) * TB])
                         for fc in range(NFC)],
                        wait=[(f"DS{ds}", dv), ("PDVE", H_ALL)], bankw=[bi])
                    if tb == NTB - 1:
                        dring.consumed(di[0] - 1, "PPE", DM)
                    wx = [("PPE", DM)]
                    XADD = dve(lambda E_, d=oc, t=tb, i=bi: E_.tensor_add(
                        xs(d, t), xs(d, t), bankt[i][:, 0:TB]), wait=wx)
                    bank_rd[bi] = ("PDVE", XADD)
                    xadd_map[(oc, tb)] = XADD
        return xadd_map

    def qkv(xn_done):
        stg4 = Slots(4)
        sti = [0]
        CCK_v = CCV_v = None
        for oc in list(range(16, 48)) + list(range(16)):
            ws, wv = wslab_load(wqkv_in[oc, :, :])
            spfx = "KQ" if oc < 16 else ("KK" if oc < 32 else "KV")
            for tb in range(NTB):
                bi = (oc * NTB + tb) % 4
                QM = mm_group(
                    bankt[bi][:, 0:TB],
                    [(wbuf[ws][:, dc * 128:(dc + 1) * 128], xns(dc, tb))
                     for dc in range(NDC)],
                    wait=[(f"WS{ws}", wv), ("PDVE", xn_done[tb])], bankw=[bi])
                if tb == NTB - 1:
                    wring.consumed(wi[0] - 1, "PPE", QM)
                i = sti[0]; sti[0] += 1
                s = stg4.slot(i)
                sg = f"{spfx}{s}"
                pw = stg4.prod_wait(i)
                wl = [("PPE", QM)] + ([pw] if pw else [])
                if oc < 16:
                    EV = dve(lambda E_, i=bi, s=s, o=oc: E_.tensor_scalar(
                        stg[:, s * TB:(s + 1) * TB], bankt[i][:, 0:TB],
                        RSCALE, bqs_sb[:, o:o + 1], ALU.mult, ALU.add), wait=wl)
                else:
                    EV = dve(lambda E_, i=bi, s=s, o=oc: E_.tensor_scalar(
                        stg[:, s * TB:(s + 1) * TB], bankt[i][:, 0:TB],
                        bq_sb[:, o:o + 1], None, ALU.add), wait=wl)
                bank_rd[bi] = ("PDVE", EV)
                if oc < 16:
                    dst = q_dram[oc * 128:(oc + 1) * 128, tb * TB:(tb + 1) * TB]
                elif oc < 32:
                    r0 = (oc - 16) * 128
                    dst = kv_contrib[r0:r0 + 128, tb * TB:(tb + 1) * TB]
                else:
                    r0 = 2048 + (oc - 32) * 128
                    dst = kv_contrib[r0:r0 + 128, tb * TB:(tb + 1) * TB]
                sgv = sy_dma(sg, dst, stg[:, s * TB:(s + 1) * TB],
                             wait=[("PDVE", EV)])
                stg4.consumed(i, sg, sgv)
            if oc == 39:
                for s in range(4):
                    b.wait("gpsimd", f"KK{s}", b.cnt[f"KK{s}"])
                CCK_v = b.step("gpsimd", "CCK", lambda E_: E_.collective_compute(
                    "AllGather", ALU.bypass, [[0, 1, 2, 3], [4, 5, 6, 7]],
                    [kv_contrib[0:2048, :].opt()], [kvgK.ap().opt()]), 1)
            if oc == 3:
                for s in range(4):
                    b.wait("gpsimd", f"KV{s}", b.cnt[f"KV{s}"])
                CCV_v = b.step("gpsimd", "CCV", lambda E_: E_.collective_compute(
                    "AllGather", ALU.bypass, [[0, 1, 2, 3], [4, 5, 6, 7]],
                    [kv_contrib[2048:4096, :].opt()], [kvgV.ap().opt()]), 1)
        for s in range(4):
            b.wait("sync", f"KQ{s}", b.cnt[f"KQ{s}"])
        return CCK_v, CCV_v

    def attention(CCK_v, CCV_v, qkv_pe_done):
        kh_cnt = [0, 0]
        zbs_rd = [None]
        punits = Slots(3)
        head_pe_done = [None, None]
        sc_bank = [0, 1, 6]
        u_glob = [0]
        AT_DONE = None
        for n in range(NH):
            buf = n % 2
            wl = []
            if head_pe_done[buf] is not None:
                wl.append(("PPE", head_pe_done[buf]))
            sy_dma(f"KH{buf}", qbuf[buf][:, :],
                   q_dram[n * 128:(n + 1) * 128, :], wait=wl)
            b.wait("sync", "CCK", CCK_v)
            for r in range(4):
                sy_dma(f"KH{buf}", kbuf[buf][:, r * 1024:(r + 1) * 1024],
                       kvgK[2048 * r + 128 * n: 2048 * r + 128 * (n + 1), :])
            b.wait("sync", "CCV", CCV_v)
            for r in range(4):
                for kb in range(8):
                    sy_dma_t(f"KH{buf}",
                             vbuf[buf][:, (r * 8 + kb) * 128:(r * 8 + kb + 1) * 128],
                             kvgV[2048 * r + 128 * n: 2048 * r + 128 * (n + 1),
                                  kb * 128:(kb + 1) * 128])
            kh_cnt[buf] += 37 * 16
            KH_READY = kh_cnt[buf]

            for qt in range(NQT):
                kbs = list(range(2 * qt, 2 * qt + 2)) + list(range(2 * qt))
                units = [(kb, r) for kb in kbs for r in range(4)]
                nu = len(units)
                npairs = nu // 2
                ob = 2 + qt % 2
                zc = (qt % 2) * NQ
                sc_of = {}
                pr_of = {}

                def emit_sc_pair(p):
                    sb = sc_bank[p % 3]
                    wl = [(f"KH{buf}", KH_READY)]
                    for h in range(2):
                        u = 2 * p + h
                        kb, r = units[u]
                        v = pe(lambda E_, sb=sb, kb=kb, r=r, bu=buf, q=qt, h=h:
                               E_.matmul(
                            bankt[sb][:, h * NQ:(h + 1) * NQ],
                            kbuf[bu][:, (r * 8 + kb) * 128:(r * 8 + kb + 1) * 128],
                            qbuf[bu][:, q * NQ:(q + 1) * NQ],
                            start=True, stop=True),
                            wait=wl if h == 0 else (), bankw=[sb] if h == 0 else ())
                    sc_of[p] = (sb, v)

                def emit_exp_mask(p):
                    sb, scv = sc_of[p]
                    ps = punits.slot(u_glob[0] + p)
                    wl = [("PPE", scv)]
                    pw = punits.prod_wait(u_glob[0] + p)
                    if pw:
                        wl.append(pw)
                    EX = act(lambda E_, sb=sb, ps=ps: E_.activation(
                        ppool[:, ps * 2 * NQ:(ps + 1) * 2 * NQ],
                        bankt[sb][:, 0:2 * NQ], AF.Exp), wait=wl)
                    bank_rd[sb] = ("PACT", EX)
                    deps = []
                    for h in range(2):
                        u = 2 * p + h
                        kb, r = units[u]
                        dep = ("PACT", EX)
                        if kb >= 2 * qt:
                            d = kb - 2 * qt
                            MK = dve(lambda E_, ps=ps, r=r, d=d, h=h: E_.tensor_mul(
                                ppool[:, (ps * 2 + h) * NQ:(ps * 2 + h + 1) * NQ],
                                ppool[:, (ps * 2 + h) * NQ:(ps * 2 + h + 1) * NQ],
                                mask_sb[:, (r * 2 + d) * NQ:(r * 2 + d + 1) * NQ]),
                                wait=[("PACT", EX)])
                            dep = ("PDVE", MK)
                        deps.append(dep)
                    pr_of[p] = deps

                def emit_avz_pair(p):
                    ps = punits.slot(u_glob[0] + p)
                    v = None
                    for h in range(2):
                        u = 2 * p + h
                        kb, r = units[u]
                        st = (u == 0)
                        sp = (u == nu - 1)
                        b.wait("tensor", *pr_of[p][h])
                        if st:
                            bank_wait("tensor", ob)
                            bank_wait("tensor", 4)
                        b.emit("tensor", lambda E_, ps=ps, kb=kb, r=r, st=st, sp=sp,
                               o=ob, bu=buf, h=h:
                               E_.matmul(bankt[o][:, 0:NQ],
                                         vbuf[bu][:, (r * 8 + kb) * 128:
                                                  (r * 8 + kb + 1) * 128],
                                         ppool[:, (ps * 2 + h) * NQ:
                                               (ps * 2 + h + 1) * NQ],
                                         start=st, stop=sp))
                        v = b.step("tensor", "PPE",
                                   lambda E_, ps=ps, st=st, sp=sp, z=zc, h=h:
                                   E_.matmul(
                                       bankt[4][0:1, z:z + NQ], ones_cb[:, :],
                                       ppool[:, (ps * 2 + h) * NQ:
                                             (ps * 2 + h + 1) * NQ],
                                       start=st, stop=sp))
                    punits.consumed(u_glob[0] + p, "PPE", v)
                    del pr_of[p]
                    return v

                for p in range(npairs + 2):
                    if p < npairs:
                        emit_sc_pair(p)
                        emit_exp_mask(p)
                    if p >= 2:
                        AVZ = emit_avz_pair(p - 2)
                u_glob[0] += npairs
                RZ = dve(lambda E_, z=zc: E_.reciprocal(R_rz[0:1, z:z + NQ],
                                                        bankt[4][0:1, z:z + NQ]),
                         wait=[("PPE", AVZ)])
                ZB = pe(lambda E_, z=zc: E_.matmul(bankt[5][:, z:z + NQ],
                                                   onesf[:, 0:128],
                                                   R_rz[:, z:z + NQ],
                                                   start=True, stop=True),
                        wait=[("PDVE", RZ)], bankw=[5])
                zw = [("PPE", ZB)]
                if zbs_rd[0] is not None:
                    zw.append(zbs_rd[0])
                ZBC = dve(lambda E_, z=zc: E_.tensor_copy(zbs[:, 0:NQ],
                                                          bankt[5][:, z:z + NQ]),
                          wait=zw)
                bank_rd[5] = ("PDVE", ZBC)
                wl = [("PDVE", ZBC)]
                if qkv_pe_done is not None:
                    wl.append(("PPE", qkv_pe_done))
                AT = dve(lambda E_, n=n, q=qt, o=ob: E_.tensor_mul(
                    xn[:, n * T + q * NQ: n * T + (q + 1) * NQ],
                    bankt[o][:, 0:NQ], zbs[:, 0:NQ]), wait=wl)
                zbs_rd[0] = ("PDVE", AT)
                bank_rd[ob] = ("PDVE", AT)
                bank_rd[4] = ("PDVE", RZ)
                AT_DONE = AT
            head_pe_done[buf] = b.cnt["PPE"]
        return AT_DONE

    def out_proj(at_done, ffn1_xadd):
        xadd_map = {}
        for oc in range(NDC):
            ws, wv = wslab_load(wo_in[oc, :, :])
            for tb in range(NTB):
                bi = (oc * NTB + tb) % 4
                OM = mm_group(
                    bankt[bi][:, 0:TB],
                    [(wbuf[ws][:, dc * 128:(dc + 1) * 128], xns(dc, tb))
                     for dc in range(NDC)],
                    wait=[(f"WS{ws}", wv), ("PDVE", at_done)], bankw=[bi])
                if tb == NTB - 1:
                    wring.consumed(wi[0] - 1, "PPE", OM)
                XADD = dve(lambda E_, i=bi, d=oc, t=tb: E_.scalar_tensor_tensor(
                    xs(d, t), bankt[i][:, 0:TB], bo_sb[:, d:d + 1], xs(d, t),
                    ALU.add, ALU.add), wait=[("PPE", OM)])
                bank_rd[bi] = ("PDVE", XADD)
                xadd_map[(oc, tb)] = XADD
        return xadd_map

    xn1 = ln(xdep_dve=None, want_router=True, call_idx=0)
    F1 = ffn(0, xn1)
    xn2 = ln(xdep_map=F1)
    CCK_v, CCV_v = qkv(xn2)
    QKV_PE = b.cnt["PPE"]
    ATD = attention(CCK_v, CCV_v, QKV_PE)
    OPm = out_proj(ATD, F1)
    xn3 = ln(xdep_map=OPm, want_router=True, call_idx=1)
    F2 = ffn(1, xn3)

    for i in range(4):
        dep = max(F2[(oc, tb)] for oc in range(4 * i, 4 * i + 4) for tb in range(NTB))
        sy_dma("DO", out_t[:, i * 4096:(i + 1) * 4096],
               x_sb[:, i * 4096:(i + 1) * 4096], wait=[("PDVE", dep)])
    b.wait("sync", "DO", b.cnt["DO"])

    with nc.Block() as block:
        @block.tensor
        def _(E):
            b.run("tensor", E)

        @block.scalar
        def _(E):
            b.run("scalar", E)

        @block.vector
        def _(E):
            b.run("vector", E)

        @block.gpsimd
        def _(E):
            b.run("gpsimd", E)

        @block.sync
        def _(E):
            b.run("sync", E)

    return nc



def _prep_shared(Wqkv, Wo, gate_w, up_w, down_w, router_w, bqkv, bo):
    f32 = np.float32
    wqkv = np.ascontiguousarray(
        np.asarray(Wqkv, f32).reshape(16, 128, 48, 128).transpose(2, 1, 0, 3)
    ).reshape(48, 128, 2048)
    wo = np.ascontiguousarray(
        np.asarray(Wo, f32).reshape(16, 128, 16, 128).transpose(2, 1, 0, 3)
    ).reshape(16, 128, 2048)
    gate = np.ascontiguousarray(
        np.asarray(gate_w, f32).reshape(16, 16, 128, 6, 128).transpose(0, 3, 2, 1, 4)
    ).reshape(16, 6, 128, 2048)
    up = np.ascontiguousarray(
        np.asarray(up_w, f32).reshape(16, 16, 128, 6, 128).transpose(0, 3, 2, 1, 4)
    ).reshape(16, 6, 128, 2048)
    down = np.ascontiguousarray(
        np.asarray(down_w, f32).reshape(16, 6, 128, 16, 128).transpose(0, 3, 2, 1, 4)
    ).reshape(16, 16, 128, 768)
    router = np.ascontiguousarray(
        np.asarray(router_w, f32).reshape(16, 128, 16).transpose(1, 0, 2)
    ).reshape(128, 256)
    bq = np.ascontiguousarray(np.asarray(bqkv, f32).reshape(48, 128).T)
    bo_t = np.ascontiguousarray(np.asarray(bo, f32).reshape(16, 128).T)
    ident = np.eye(128, dtype=f32)
    return wqkv, wo, gate, up, down, router, bq, bo_t, ident


def _masks_for_core(c):
    m = np.zeros((128, 4, 2, 256), dtype=np.float32)
    s = np.arange(128)[:, None]
    j = np.arange(256)[None, :]
    for r in range(4):
        off = 0 if c >= r else 1
        for d in range(2):
            m[:, r, d, :] = (j >= 128 * d + s + off).astype(np.float32)
    return m.reshape(128, 2048).astype(ml_dtypes.bfloat16)


def kernel(L0_attn_x, L0_attn_Wqkv, L0_attn_bqkv, L0_attn_Wo, L0_attn_bo,
           L0_ffn_router_w, L0_ffn_gate_w, L0_ffn_up_w, L0_ffn_down_w):
    x = np.asarray(L0_attn_x, np.float32)
    wqkv, wo, gate, up, down, router, bq, bo_t, ident = _prep_shared(
        L0_attn_Wqkv, L0_attn_Wo, L0_ffn_gate_w, L0_ffn_up_w, L0_ffn_down_w,
        L0_ffn_router_w, L0_attn_bqkv, L0_attn_bo)

    nc = build()
    in_maps = []
    for core in range(8):
        g, c = core // 4, core % 4
        xs_ = x[g, c::4, :]
        x_fm = np.ascontiguousarray(
            xs_.T.reshape(16, 128, T).transpose(1, 0, 2)).reshape(128, NDC * T)
        in_maps.append({
            "x": x_fm, "wqkv": wqkv, "wo": wo, "gate": gate, "up": up,
            "down": down, "router": router, "bqkv": bq, "bo": bo_t,
            "identw": ident, "masks": _masks_for_core(c),
        })
    res = run_bass_kernel_spmd(nc, in_maps, core_ids=list(range(8)))
    out = np.empty((2, 4096, D), dtype=np.float32)
    for core in range(8):
        g, c = core // 4, core % 4
        o = np.asarray(res.results[core]["out"])
        fm = o.reshape(128, NDC, T).transpose(1, 0, 2).reshape(D, T)
        out[g, c::4, :] = fm.T
    return out


if __name__ == "__main__":
    import time
    t0 = time.time()
    nc = build()
    print(f"built in {time.time()-t0:.1f}s, n_instructions={nc.n_instructions()}")
